# version 02
# baseline (speedup 1.0000x reference)
import contextlib
import numpy as np
import ml_dtypes
import concourse.bass as bass
import concourse.mybir as mybir
from concourse.bass_utils import run_bass_kernel_spmd

F32 = mybir.dt.float32
BF16 = mybir.dt.bfloat16
U8 = mybir.dt.uint8
AF = mybir.ActivationFunctionType
ALU = mybir.AluOpType
AX = mybir.AxisListType
BF = ml_dtypes.bfloat16

T, D = 2048, 1024
NT, ND, NPAIR = 16, 8, 8
F0, NFC, NFH = 4096, 32, 16
E = 8
LNEPS = 1e-5
ATTN0_EPS = 1e-6
L1FFN_EPS = 1e-4
W_THRES = 0.1
NEG = -1e30
PHASES = ["ln0", "attn0", "ffn0", "res0", "ln1", "attn1", "ffn1", "final"]


def build(upto="final", debug=False):
    nc = bass.Bass("TRN2", target_bir_lowering=False)
    dt = nc.dram_tensor
    x_h = dt("x", [T, D], F32, kind="ExternalInput").ap()
    wq_h = dt("wq_t", [NPAIR, 128, ND, 128], F32, kind="ExternalInput").ap()
    wkv_h = dt("wkv_t", [NPAIR, 128, ND, 256], F32, kind="ExternalInput").ap()
    bq_h = dt("bq_t", [64, 16], F32, kind="ExternalInput").ap()
    bkv_h = dt("bkv", [NPAIR, 256], F32, kind="ExternalInput").ap()
    wo_h = dt("wo_t", [D, D], F32, kind="ExternalInput").ap()
    bo_h = dt("bo", [1, D], F32, kind="ExternalInput").ap()
    sww_h = dt("sw_w", [D, E], F32, kind="ExternalInput").ap()
    swb_h = dt("sw_b", [1, E], F32, kind="ExternalInput").ap()
    w1_h = dt("w1_t", [E, NFC, 128, ND, 128], F32, kind="ExternalInput").ap()
    b1_h = dt("b1_t", [E, 128, 32], F32, kind="ExternalInput").ap()
    w2_h = dt("w2_t", [E, 2, 128, NFH, D], F32, kind="ExternalInput").ap()
    b2_h = dt("b2", [E, D], F32, kind="ExternalInput").ap()
    wqkv_h = dt("wqkv_t", [NPAIR, 128, ND, 384], F32, kind="ExternalInput").ap()
    bqkv_h = dt("bqkv", [NPAIR, 384], F32, kind="ExternalInput").ap()
    wout_h = dt("wout_t", [D, D], F32, kind="ExternalInput").ap()
    bout_h = dt("bout", [1, D], F32, kind="ExternalInput").ap()
    gw_h = dt("gw", [D, E], F32, kind="ExternalInput").ap()
    gb_h = dt("gb", [1, E], F32, kind="ExternalInput").ap()
    ew_h = dt("ew_t", [E, 128, ND, D], F32, kind="ExternalInput").ap()
    eb_h = dt("eb", [E, D], F32, kind="ExternalInput").ap()
    idb_h = dt("idb", [128, 128], BF16, kind="ExternalInput").ap()
    tri_h = dt("trimask", [128, 128], BF16, kind="ExternalInput").ap()
    cos_h = dt("cosb", [128, NT, 64], BF16, kind="ExternalInput").ap()
    sin_h = dt("sinb", [128, NT, 64], BF16, kind="ExternalInput").ap()
    idx8_h = dt("idx8", [128, E], F32, kind="ExternalInput").ap()
    y_h = dt("y", [T, D], F32, kind="ExternalOutput").ap()
    y_v = y_h.rearrange("(a p) f -> p a f", p=128)
    x_v = x_h.rearrange("(a p) f -> p a f", p=128)
    dbg = {}
    if debug:
        dbg["xn"] = dt("dbg_xn", [128, ND, T], F32, kind="ExternalOutput").ap()
        dbg["rb"] = dt("dbg_rb", [128, NT, D], F32, kind="ExternalOutput").ap()
        dbg["xb"] = dt("dbg_xb", [128, NT, D], F32, kind="ExternalOutput").ap()

    es = contextlib.ExitStack()
    sb = lambda *a: es.enter_context(nc.sbuf_tensor(*a))
    BIG1 = sb("BIG1", [128, 65536], U8)
    RB = sb("RB", [128, NT, D], F32)
    XN = sb("XN", [128, ND, T], BF16)
    DA = sb("DA", [128, 32768], U8)
    WS = sb("WS", [128, 8192], U8)
    idb = sb("idb_s", [128, 128], BF16)
    tri = sb("tri_s", [128, 128], BF16)
    cosb = sb("cosb_s", [128, NT, 64], BF16)
    sinb = sb("sinb_s", [128, NT, 64], BF16)
    ones_b = sb("ones_b", [1, 128], BF16)
    ones_f = sb("ones_f", [128, 64], F32)
    bqH = sb("bqH", [64, 16], F32)
    b1t = sb("b1t", [128, 32], F32)
    gt = sb("gt", [128, NT, E], F32)
    gbf = sb("gbf", [128, NT, E], BF16)
    idx8 = sb("idx8_s", [128, E], F32)
    stats = sb("stats", [128, 16], F32)
    r8a = sb("r8a", [128, 8], F32)
    r8b = sb("r8b", [128, 8], F32)
    r8c = sb("r8c", [128, 8], F32)
    brow = sb("brow", [1, 768], BF16)

    HB = BIG1[:, :].bitcast(BF16).rearrange("p (c t) -> p c t", t=T)
    XB = BIG1[:, :].bitcast(F32).rearrange("p (c f) -> p c f", f=D)

    def carve(base, off, nbytes, dtype, inner=None, p0=0, p1=128):
        v = base[p0:p1, off:off + nbytes].bitcast(dtype)
        if inner is not None:
            v = v.rearrange("p (a b) -> p a b", b=inner)
        return v

    _uid = [0]

    def BLK(name):
        _uid[0] += 1
        return nc.Block(f"{name}{_uid[0]}")

    def PS(name, shape, dtype):
        _uid[0] += 1
        return nc.psum_tensor(f"{name}_{_uid[0]}", shape, dtype)

    sem = {}
    cnt = {}
    for name in ["pe", "pe2", "act", "act2", "dve", "dve2", "dve3", "gd", "gd2",
                 "gd3", "sd", "sd2", "aux", "ch_dve", "ch_act", "ch_pe", "ch_pool",
                 "ch_sp"]:
        sem[name] = es.enter_context(nc.semaphore(name))
        cnt[name] = 0

    def inc(instr, name, n=1):
        instr.then_inc(sem[name], n)
        cnt[name] += n

    _chmap = {}

    def CH(eng, instr):
        key = id(eng)
        if key not in _chmap:
            n = type(eng).__name__
            _chmap[key] = ("ch_dve" if "Vector" in n else
                           "ch_act" if "Scalar" in n else
                           "ch_pe" if "Tensor" in n else
                           "ch_pool" if "GpSimd" in n else "ch_sp")
        name = _chmap[key]
        instr.then_inc(sem[name], 1)
        cnt[name] += 1
        eng.wait_ge(sem[name], cnt[name])

    wq_s = carve(WS, 0, 2048, BF16, 128)
    wkv_s = carve(WS, 2048, 4096, BF16, 256)

    with BLK("consts") as blk:
        @blk.sync
        def _(s):
            for dst, src in [(idb[:], idb_h), (tri[:], tri_h), (cosb[:], cos_h),
                             (sinb[:], sin_h), (bqH[:], bq_h), (idx8[:], idx8_h)]:
                inc(s.dma_start(out=dst, in_=src), "sd", 16)
            s.wait_ge(sem["sd"], cnt["sd"])


        @blk.vector
        def _(v):
            v.memset(ones_b[:], 1.0)
            v.memset(ones_f[:], 1.0)
            if debug:
                v.memset(RB[:, :, :].rearrange("p a b -> p (a b)"), 0.0)
                v.memset(XB[:, :, :].rearrange("p a b -> p (a b)"), 0.0)

    PARK0 = BIG1[:, 0:32768].bitcast(BF16).rearrange("p (c t) -> p c t", t=T)
    PARK1 = DA[:, 0:32768].bitcast(BF16).rearrange("p (c t) -> p c t", t=T)

    def emit_ln(src_kind, dst_lo):
        xst = carve(DA, 0, 8192, F32, D) if src_kind == "hbm" else None
        xn32 = carve(WS, 0, 4096, F32)
        xnt_h = carve(WS, 4096, 2048, BF16)
        xnt_l = carve(WS, 6144, 2048, BF16)
        b_sd, b_act, b_act2 = cnt["sd"], cnt["act"], cnt["act2"]
        b_dve, b_dve2, b_dve3, b_pe = cnt["dve"], cnt["dve2"], cnt["dve3"], cnt["pe"]
        with PS("psT", [128, 2048], BF16) as psT, BLK("ln") as blk:
            b_sd2 = cnt["sd2"]
            if src_kind == "hbm":
                @blk.sync
                def _(s):
                    for tt in range(NT):
                        if tt >= 2:
                            s.wait_ge(sem["dve2"], b_dve2 + tt - 1)
                        sp = "sd" if tt % 2 == 0 else "sd2"
                        inc(s.dma_start(out=xst[:, tt % 2, :], in_=x_v[:, tt, :]), sp, 16)
                    s.wait_ge(sem["sd"], cnt["sd"])
                    s.wait_ge(sem["sd2"], cnt["sd2"])

            def src(tt):
                return xst[:, tt % 2, :] if src_kind == "hbm" else XB[:, tt, :]

            @blk.scalar
            def _(sc):
                for tt in range(NT):
                    if src_kind == "hbm":
                        sp, bb = ("sd", b_sd) if tt % 2 == 0 else ("sd2", b_sd2)
                        sc.wait_ge(sem[sp], bb + (tt // 2 + 1) * 16)
                    if tt >= 1:
                        sc.wait_ge(sem["dve"], b_dve + tt)
                        sc.wait_ge(sem["pe"], b_pe + tt)
                    sc.activation(out=xnt_h[:, :], in_=src(tt), func=AF.Square,
                                  accum_out=stats[:, 1:2]).then_inc(sem["act"], 1)
                    sc.wait_ge(sem["dve"], b_dve + tt + 1)
                    mm = sc.activation(out=stats[:, 6:7], in_=stats[:, 5:6], func=AF.Ln)
                    CH(sc, mm)
                    sc.activation(out=stats[:, 7:8], in_=stats[:, 6:7], func=AF.Exp,
                                  scale=-0.5).then_inc(sem["act2"], 1)
            cnt["act"] += NT
            cnt["act2"] += NT

            @blk.vector
            def _(v):
                for tt in range(NT):
                    if tt >= 1:
                        j = tt - 1
                        v.wait_ge(sem["pe"], b_pe + j + 1)
                        v.tensor_copy(
                            out=XN[:, :, j * 128:(j + 1) * 128],
                            in_=psT[:, 0:1024].rearrange("p (c t) -> p c t", t=128))
                        v.tensor_copy(
                            out=dst_lo[:, :, j * 128:(j + 1) * 128],
                            in_=psT[:, 1024:2048].rearrange("p (c t) -> p c t", t=128)
                        ).then_inc(sem["dve3"], 1)
                    if src_kind == "hbm":
                        sp, bb = ("sd", b_sd) if tt % 2 == 0 else ("sd2", b_sd2)
                        v.wait_ge(sem[sp], bb + (tt // 2 + 1) * 16)
                    mm = v.reduce_sum(out=stats[:, 0:1], in_=src(tt), axis=AX.X)
                    CH(v, mm)
                    v.tensor_scalar(out=stats[:, 2:3], in0=stats[:, 0:1],
                                    scalar1=1.0 / D, scalar2=None, op0=ALU.mult)
                    v.wait_ge(sem["act"], b_act + tt + 1)
                    mm = v.tensor_scalar(out=stats[:, 3:4], in0=stats[:, 1:2],
                                         scalar1=1.0 / D, scalar2=None, op0=ALU.mult)
                    CH(v, mm)
                    mm = v.scalar_tensor_tensor(out=stats[:, 4:5], in0=stats[:, 2:3],
                                                scalar=stats[:, 2:3], in1=stats[:, 3:4],
                                                op0=ALU.mult, op1=ALU.subtract)
                    CH(v, mm)
                    v.tensor_scalar(out=stats[:, 5:6], in0=stats[:, 4:5], scalar1=-1.0,
                                    scalar2=LNEPS, op0=ALU.mult, op1=ALU.add
                                    ).then_inc(sem["dve"], 1)
                    v.wait_ge(sem["act2"], b_act2 + tt + 1)
                    mm = v.tensor_tensor(out=stats[:, 8:9], in0=stats[:, 2:3],
                                         in1=stats[:, 7:8], op=ALU.mult)
                    CH(v, mm)
                    mm = v.tensor_scalar(out=xn32[:, :], in0=src(tt),
                                         scalar1=stats[:, 7:8], scalar2=stats[:, 8:9],
                                         op0=ALU.mult, op1=ALU.subtract)
                    CH(v, mm)
                    mm = v.tensor_copy(out=xnt_h[:, :], in_=xn32[:, :])
                    CH(v, mm)
                    v.tensor_tensor(out=xnt_l[:, :], in0=xn32[:, :], in1=xnt_h[:, :],
                                    op=ALU.subtract).then_inc(sem["dve2"], 1)
                j = NT - 1
                v.wait_ge(sem["pe"], b_pe + j + 1)
                v.tensor_copy(out=XN[:, :, j * 128:(j + 1) * 128],
                              in_=psT[:, 0:1024].rearrange("p (c t) -> p c t", t=128))
                v.tensor_copy(out=dst_lo[:, :, j * 128:(j + 1) * 128],
                              in_=psT[:, 1024:2048].rearrange("p (c t) -> p c t", t=128)
                              ).then_inc(sem["dve3"], 1)
            cnt["dve"] += NT
            cnt["dve2"] += NT
            cnt["dve3"] += NT

            @blk.tensor
            def _(t):
                for tt in range(NT):
                    t.wait_ge(sem["dve2"], b_dve2 + tt + 1)
                    if tt >= 1:
                        t.wait_ge(sem["dve3"], b_dve3 + tt)
                    for dc in range(ND):
                        t.transpose(psT[:, dc * 128:(dc + 1) * 128],
                                    xnt_h[:, dc * 128:(dc + 1) * 128], idb[:])
                    for dc in range(ND):
                        mm = t.transpose(psT[:, 1024 + dc * 128:1024 + (dc + 1) * 128],
                                         xnt_l[:, dc * 128:(dc + 1) * 128], idb[:])
                    mm.then_inc(sem["pe"], 1)
            cnt["pe"] += NT

    def emit_attn0():
        import os
        _sub = os.environ.get("A0_SUB", "abcd")
        wready = {}
        Qfm = carve(DA, 0, 8192, BF16, 2048, p1=64)
        KVb = carve(DA, 8192, 264, BF16, p1=64)
        Afm = carve(DA, 8704, 8192, BF16, p1=64)
        phiT = carve(DA, 16896, 4096, BF16, p1=64)
        invb = carve(DA, 16896, 8192, F32, p1=64)
        invr = carve(DA, 0, 8192, F32, p0=64, p1=65)
        Ktm = carve(DA, 20992, 4096, BF16, 128)
        Vext = carve(DA, 25088, 4224, BF16, 132)
        ktmp = carve(DA, 29312, 768, BF16, 128)
        borow = carve(DA, 30080, 2048, BF16, p1=1)
        wo0 = carve(DA, 0, 4096, BF16, 1024, p1=64)

        for p in range(NPAIR):
            pk = (p % 2) * 256
            b_pe, b_act, b_dve = cnt["pe"], cnt["act"], cnt["dve"]
            b_dve2, b_dve3, b_gd = cnt["dve2"], cnt["dve3"], cnt["gd"]
            with (PS("psA", [64, 2048], F32) as psA,
                  PS("psB0", [128, 512], F32) as psB0,
                  PS("psB1", [128, 512], F32) as psB1,
                  BLK("a0a") as blk):
                psB = [psB0, psB1]

                @blk.gpsimd
                def _(g):
                    if p == 0:
                        inc(g.dma_start(out=wq_s, in_=wq_h[0]), "gd", 16)
                        inc(g.dma_start(out=wkv_s, in_=wkv_h[0]), "gd", 16)
                        inc(g.dma_start(out=brow[0:1, 0:256], in_=bkv_h[0:1, :]),
                            "gd", 16)
                        wready[0] = cnt["gd"]
                        inc(g.dma_start(out=borow, in_=bo_h), "gd", 16)
                    if p + 1 < NPAIR:
                        g.wait_ge(sem["pe"], b_pe + 2 + NT)
                        inc(g.dma_start(out=wq_s, in_=wq_h[p + 1]), "gd", 16)
                        inc(g.dma_start(out=wkv_s, in_=wkv_h[p + 1]), "gd", 16)
                        inc(g.dma_start(out=brow[0:1, 256 - pk:512 - pk],
                                        in_=bkv_h[p + 1:p + 2, :]), "gd", 16)
                    g.wait_ge(sem["gd"], cnt["gd"])

                @blk.vector
                def _(v):
                    v.memset(Vext[:, :, :].rearrange("p a b -> p (a b)"), 1.0)
                    nop = v.nop()
                    nop.then_inc(sem["dve2"], 1)
                cnt["dve2"] += 1

                @blk.tensor
                def _(t):
                    if p == 0:
                        t.wait_ge(sem["gd"], wready[0])
                    for hq in range(2):
                        if hq == 1:
                            t.wait_ge(sem["dve3"], b_dve3 + 2)
                        for tb in range(4):
                            for dc in range(ND):
                                mm = t.matmul(psA[:, tb * 512:(tb + 1) * 512],
                                              wq_s[:, dc, 64 * hq:64 * hq + 64],
                                              XN[:, dc, tb * 512:(tb + 1) * 512],
                                              start=(dc == 0), stop=(dc == ND - 1))
                        mm.then_inc(sem["pe"], 1)
                    cnt["pe"] += 2
                    t.wait_ge(sem["dve2"], b_dve2 + 1)
                    for tt in range(NT):
                        if tt >= 2:
                            t.wait_ge(sem["dve"], b_dve + 2 * (tt - 2) + 2)
                        pb = psB[tt % 2][:, 0:256]
                        for dc in range(ND):
                            t.matmul(pb, XN[:, dc, tt * 128:(tt + 1) * 128],
                                     wkv_s[:, dc, :], start=(dc == 0), stop=False)
                        mm = t.matmul(pb, ones_b[:], brow[0:1, pk:pk + 256],
                                      start=False, stop=True)
                        mm.then_inc(sem["pe"], 1)
                    cnt["pe"] += NT

                @blk.vector
                def _(v):
                    for hq in range(2):
                        v.wait_ge(sem["pe"], b_pe + hq + 1)
                        mm = v.tensor_scalar(out=phiT[:, :], in0=psA[:, :],
                                             scalar1=bqH[:, 2 * p + hq:2 * p + hq + 1],
                                             scalar2=0.0, op0=ALU.add, op1=ALU.min)
                        inc(mm, "dve3")
                        v.wait_ge(sem["dve3"], cnt["dve3"])
                        mm = v.tensor_scalar(out=psA[:, :], in0=psA[:, :],
                                             scalar1=bqH[:, 2 * p + hq:2 * p + hq + 1],
                                             scalar2=0.0, op0=ALU.add, op1=ALU.max)
                        CH(v, mm)
                        v.wait_ge(sem["act"], b_act + hq + 1)
                        mm = v.tensor_tensor(out=Qfm[:, hq, :], in0=Qfm[:, hq, :],
                                             in1=psA[:, :], op=ALU.add)
                        inc(mm, "dve3")
                    for tt in range(NT):
                        v.wait_ge(sem["pe"], b_pe + 2 + tt + 1)
                        pb = psB[tt % 2][:, 0:256]
                        mm = v.tensor_scalar(out=ktmp[:, tt % 2, :], in0=pb[:, 0:128],
                                             scalar1=0.0, scalar2=None, op0=ALU.min)
                        inc(mm, "dve")
                        v.wait_ge(sem["dve"], cnt["dve"])
                        v.tensor_scalar(out=pb[:, 0:128], in0=pb[:, 0:128],
                                        scalar1=0.0, scalar2=None, op0=ALU.max)
                        v.tensor_copy(out=Vext[:, tt, 0:64], in_=pb[:, 128:192])
                        v.tensor_copy(out=Vext[:, tt, 66:130], in_=pb[:, 192:256])
                        v.wait_ge(sem["act"], b_act + 2 + tt + 1)
                        mm = v.tensor_tensor(out=Ktm[:, tt, :], in0=ktmp[:, 2, :],
                                             in1=pb[:, 0:128], op=ALU.add)
                        inc(mm, "dve")

                @blk.scalar
                def _(sc):
                    for hq in range(2):
                        sc.wait_ge(sem["dve3"], b_dve3 + 2 * hq + 1)
                        sc.activation(out=Qfm[:, hq, :], in_=phiT[:, :], func=AF.Exp
                                      ).then_inc(sem["act"], 1)
                    for tt in range(NT):
                        sc.wait_ge(sem["dve"], b_dve + 2 * tt + 1)
                        sc.activation(out=ktmp[:, 2, :], in_=ktmp[:, tt % 2, :],
                                      func=AF.Exp).then_inc(sem["act"], 1)
                    cnt["act"] += NT + 2

            if "b" not in _sub:
                continue
            b_pe = cnt["pe"]
            with (PS("psC", [64, 512], F32) as psC,
                  BLK("a0b") as blk):
                @blk.tensor
                def _(t):
                    for hp in range(2):
                        for tt in range(NT):
                            mm = t.matmul(psC[:, 66 * hp:66 * hp + 65],
                                          Ktm[:, tt, 64 * hp:64 * hp + 64],
                                          Vext[:, tt, 66 * hp:66 * hp + 65],
                                          start=(tt == 0), stop=(tt == NT - 1))
                    mm.then_inc(sem["pe"], 1)
                cnt["pe"] += 1

                @blk.vector
                def _(v):
                    v.wait_ge(sem["pe"], b_pe + 1)
                    v.tensor_copy(out=KVb[:, 0:65], in_=psC[:, 0:65])
                    v.tensor_copy(out=KVb[:, 66:131], in_=psC[:, 66:131])

            if "c" not in _sub:
                continue
            b_pe, b_dve, b_dve2 = cnt["pe"], cnt["dve"], cnt["dve2"]
            with BLK("a0c") as blk:
                @blk.gpsimd
                def _(g):
                    g.wait_ge(sem["pe"], b_pe + 3)
                    inc(g.dma_start(
                        out=wo0,
                        in_=wo_h[p * 128:(p + 1) * 128, :].rearrange(
                            "(h a) f -> a h f", h=2)), "gd", 16)
                    g.wait_ge(sem["gd"], cnt["gd"])

                with (PS("psO", [128, 2048], F32) as psO,
                      PS("psE", [64, 2048], F32) as psE):
                    @blk.tensor
                    def _(t):
                        for hp in range(2):
                            if hp == 1:
                                t.wait_ge(sem["dve"], b_dve + 2)
                            for tb in range(4):
                                mm = t.matmul(psO[0:65, tb * 512:(tb + 1) * 512],
                                              KVb[:, 66 * hp:66 * hp + 65],
                                              Qfm[:, hp, tb * 512:(tb + 1) * 512],
                                              start=True, stop=True)
                            mm.then_inc(sem["pe"], 1)
                            t.wait_ge(sem["dve2"], b_dve2 + hp + 1)
                            for tb in range(4):
                                mm = t.matmul(psE[:, tb * 512:(tb + 1) * 512],
                                              ones_f[64:65, :],
                                              invr[0:1, tb * 512:(tb + 1) * 512],
                                              start=True, stop=True)
                            mm.then_inc(sem["pe"], 1)
                        cnt["pe"] += 4

                    @blk.vector
                    def _(v):
                        for hp in range(2):
                            v.wait_ge(sem["pe"], b_pe + 2 * hp + 1)
                            mm = v.tensor_scalar(out=invr[0:1, :], in0=psO[64:65, :],
                                                 scalar1=ATTN0_EPS, scalar2=None,
                                                 op0=ALU.add)
                            CH(v, mm)
                            mm = v.reciprocal(out=invr[0:1, :], in_=invr[0:1, :])
                            inc(mm, "dve2")
                            v.wait_ge(sem["pe"], b_pe + 2 * hp + 2)
                            mm = v.tensor_copy(out=invb[:, :], in_=psE[:, :])
                            CH(v, mm)
                            mm = v.tensor_tensor(out=Afm[:, hp * 2048:(hp + 1) * 2048],
                                                 in0=psO[0:64, :], in1=invb[:, :],
                                                 op=ALU.mult)
                            inc(mm, "dve")
                            nop = v.nop()
                            inc(nop, "dve")

                if "d" not in _sub:
                    continue
                b_pe2, b_dve3 = cnt["pe2"], cnt["dve3"]
                with (PS("psF0", [128, 1024], F32) as psF0,
                      PS("psF1", [128, 1024], F32) as psF1):
                    psF = [psF0, psF1]

                    @blk.tensor
                    def _(t):
                        t.wait_ge(sem["dve"], b_dve + 4)
                        t.wait_ge(sem["gd"], cnt["gd"])
                        for tt in range(NT):
                            if tt >= 2:
                                t.wait_ge(sem["dve3"], b_dve3 + tt - 1)
                            pf = psF[tt % 2]
                            for nb in range(2):
                                for hp in range(2):
                                    mm = t.matmul(pf[:, nb * 512:(nb + 1) * 512],
                                                  Afm[:, hp * 2048 + tt * 128:
                                                      hp * 2048 + (tt + 1) * 128],
                                                  wo0[:, hp, nb * 512:(nb + 1) * 512],
                                                  start=(hp == 0),
                                                  stop=(hp == 1 and p != 0))
                                if p == 0:
                                    mm = t.matmul(pf[:, nb * 512:(nb + 1) * 512],
                                                  ones_b[:],
                                                  borow[0:1, nb * 512:(nb + 1) * 512],
                                                  start=False, stop=True)
                            mm.then_inc(sem["pe2"], 1)
                        cnt["pe2"] += NT

                    @blk.vector
                    def _(v):
                        for tt in range(NT):
                            v.wait_ge(sem["pe2"], b_pe2 + tt + 1)
                            pf = psF[tt % 2]
                            if p == 0:
                                v.tensor_copy(out=RB[:, tt, :], in_=pf[:]
                                              ).then_inc(sem["dve3"], 1)
                            else:
                                v.tensor_tensor(out=RB[:, tt, :], in0=RB[:, tt, :],
                                                in1=pf[:], op=ALU.add
                                                ).then_inc(sem["dve3"], 1)
                    cnt["dve3"] += NT

    def emit_ffn0():
        w1ready = {}
        w2ready = {}
        w2h = carve(DA, 0, 32768, BF16, D)
        w1buf = [carve(WS, 0, 2048, BF16, 128), carve(WS, 2048, 2048, BF16, 128)]
        sws = carve(WS, 4096, 128, BF16, 8)

        swsf = carve(WS, 0, 256, F32, 8)
        sws_lo = carve(WS, 4224, 128, BF16, 8)
        b_pe, b_gd, b_act0, b_dve2 = cnt["pe"], cnt["gd"], cnt["act"], cnt["dve2"]
        b_aux = cnt["aux"]
        with (PS("psR0", [128, 512], F32) as psR0,
              PS("psR1", [128, 512], F32) as psR1,
              BLK("route0") as blk):
            psR = [psR0, psR1]

            @blk.gpsimd
            def _(g):
                inc(g.dma_start(out=swsf,
                                in_=sww_h.rearrange("(c p) e -> p c e", p=128)),
                    "gd", 16)
                inc(g.dma_start(out=brow[0:1, 0:8], in_=swb_h), "gd", 16)
                g.wait_ge(sem["gd"], cnt["gd"])

            @blk.sync
            def _(s):
                inc(s.dma_start(out=b1t[:], in_=b1_h[0]), "sd", 16)
                s.wait_ge(sem["sd"], cnt["sd"])

            @blk.tensor
            def _(t):
                t.wait_ge(sem["gd"], b_gd + 32)
                t.wait_ge(sem["aux"], b_aux + 1)
                for tt in range(NT):
                    if tt >= 2:
                        t.wait_ge(sem["act"], b_act0 + tt - 1)
                        t.wait_ge(sem["dve2"], b_dve2 + tt - 1)
                    pr = psR[tt % 2][:, 0:8]
                    for dc in range(ND):
                        t.matmul(pr, XN[:, dc, tt * 128:(tt + 1) * 128], sws[:, dc, :],
                                 start=(dc == 0), stop=False)
                    for dc in range(ND):
                        t.matmul(pr, PARK0[:, dc, tt * 128:(tt + 1) * 128],
                                 sws[:, dc, :], start=False, stop=False)
                    for dc in range(ND):
                        t.matmul(pr, XN[:, dc, tt * 128:(tt + 1) * 128],
                                 sws_lo[:, dc, :], start=False, stop=False)
                    mm = t.matmul(pr, ones_b[:], brow[0:1, 0:8], start=False, stop=True)
                    mm.then_inc(sem["pe"], 1)
                cnt["pe"] += NT

            @blk.vector
            def _(v):
                v.wait_ge(sem["gd"], b_gd + 16)
                mm = v.tensor_copy(out=sws[:, :, :], in_=swsf[:, :, :])
                CH(v, mm)
                mm = v.tensor_tensor(out=sws_lo[:, :, :], in0=swsf[:, :, :],
                                     in1=sws[:, :, :], op=ALU.subtract)
                inc(mm, "aux")
                for tt in range(NT):
                    v.wait_ge(sem["pe"], b_pe + tt + 1)
                    pr = psR[tt % 2][:, 0:8]
                    mm = v.reduce_max(out=stats[:, 0:1], in_=pr, axis=AX.X, negate=True)
                    CH(v, mm)
                    mm = v.tensor_scalar(out=stats[:, 1:2], in0=stats[:, 0:1],
                                         scalar1=-1.0, scalar2=None, op0=ALU.mult)
                    inc(mm, "dve")
                    v.wait_ge(sem["act"], b_act0 + tt + 1)
                    mm = v.reciprocal(out=stats[:, 3:4], in_=stats[:, 2:3])
                    CH(v, mm)
                    mm = v.tensor_scalar(out=r8b[:], in0=pr, scalar1=stats[:, 1:2],
                                         scalar2=None, op0=ALU.is_ge)
                    CH(v, mm)
                    mm = v.tensor_scalar(out=gt[:, tt, :], in0=r8b[:],
                                         scalar1=stats[:, 3:4], scalar2=None,
                                         op0=ALU.mult)
                    CH(v, mm)
                    mm = v.tensor_copy(out=gbf[:, tt, :], in_=gt[:, tt, :])
                    inc(mm, "dve2")

            b_dve_r = cnt["dve"] - NT

            @blk.scalar
            def _(sc):
                for tt in range(NT):
                    sc.wait_ge(sem["dve"], b_dve_r + tt + 1)
                    sc.activation(out=r8a[:], in_=psR[tt % 2][:, 0:8], func=AF.Exp,
                                  bias=stats[:, 0:1], accum_out=stats[:, 2:3]
                                  ).then_inc(sem["act"], 1)
            cnt["act"] += NT

        for e in range(E):
            for fh in range(2):
                it = e * 2 + fh
                b_pe, b_act, b_dve, b_gd = cnt["pe"], cnt["act"], cnt["dve"], cnt["gd"]
                b_sd = cnt["sd"]
                with (PS("psH0", [128, 1024], F32) as psH0,
                      PS("psH1", [128, 1024], F32) as psH1,
                      PS("psY0", [128, 1024], F32) as psY0,
                      PS("psY1", [128, 1024], F32) as psY1,
                      BLK("ffn0") as blk):
                    psH = [psH0, psH1]
                    psY = [psY0, psY1]
                    ne, nfh = (e, fh + 1) if fh == 0 else (e + 1, 0)
                    n_w2h = 0 if it == 0 else 1

                    @blk.gpsimd
                    def _(g):
                        if it == 0:
                            inc(g.dma_start(out=w1buf[0], in_=w1_h[0, 0]), "gd", 16)
                            w1ready[0] = ("gd", cnt["gd"])
                            inc(g.dma_start(out=w1buf[1], in_=w1_h[0, 1]), "gd2", 16)
                            w1ready[1] = ("gd2", cnt["gd2"])
                        inc(g.dma_start(out=w2h, in_=w2_h[e, fh]), "gd3", 16)
                        w2ready[it] = cnt["gd3"]
                        for k in range(2, NFH + 2):
                            g.wait_ge(sem["pe"], b_pe + 2 * min(k, NFH))
                            if k < NFH:
                                srcw = w1_h[e, fh * NFH + k]
                            elif ne < E:
                                srcw = w1_h[ne, nfh * NFH + (k - NFH)]
                            else:
                                break
                            sp = "gd" if k % 2 == 0 else "gd2"
                            inc(g.dma_start(out=w1buf[k % 2], in_=srcw), sp, 16)
                            w1ready[it * NFH + k] = (sp, cnt[sp])

                    @blk.sync
                    def _(s):
                        if fh == 1 and e + 1 < E:
                            s.wait_ge(sem["act"], b_act + 2 * NFH)
                            inc(s.dma_start(out=b1t[:], in_=b1_h[e + 1]), "sd", 16)
                            s.wait_ge(sem["sd"], cnt["sd"])

                    @blk.tensor
                    def _(t):
                        for u in range(2 * NFH):
                            k, half = u // 2, u % 2
                            if half == 0:
                                spn, cv = w1ready[it * NFH + k]
                                t.wait_ge(sem[spn], cv)
                            if u >= 2:
                                t.wait_ge(sem["act"], b_act + u - 1)
                            ph = psH[u % 2]
                            for tb in range(2):
                                c0 = half * 1024 + tb * 512
                                for dc in range(ND):
                                    mm = t.matmul(ph[:, tb * 512:(tb + 1) * 512],
                                                  w1buf[k % 2][:, dc, :],
                                                  XN[:, dc, c0:c0 + 512],
                                                  start=(dc == 0), stop=(dc == ND - 1))
                            mm.then_inc(sem["pe"], 1)
                        cnt["pe"] += 2 * NFH
                        t.wait_ge(sem["act"], b_act + 2 * NFH)
                        t.wait_ge(sem["gd3"], w2ready[it])
                        for tt in range(NT):
                            if tt >= 2:
                                t.wait_ge(sem["dve"], b_dve + tt - 1)
                            py = psY[tt % 2]
                            for nb in range(2):
                                for k in range(NFH):
                                    mm = t.matmul(
                                        py[:, nb * 512:(nb + 1) * 512],
                                        HB[:, k, tt * 128:(tt + 1) * 128],
                                        w2h[:, k, nb * 512:(nb + 1) * 512],
                                        start=(k == 0), stop=(k == NFH - 1))
                            mm.then_inc(sem["pe"], 1)
                        cnt["pe"] += NT

                    @blk.scalar
                    def _(sc):
                        for u in range(2 * NFH):
                            k, half = u // 2, u % 2
                            sc.wait_ge(sem["pe"], b_pe + u + 1)
                            sc.activation(out=HB[:, k, half * 1024:(half + 1) * 1024],
                                          in_=psH[u % 2][:],
                                          func=AF.Gelu_apprx_tanh,
                                          bias=b1t[:, fh * NFH + k:fh * NFH + k + 1]
                                          ).then_inc(sem["act"], 1)
                        cnt["act"] += 2 * NFH

                    @blk.vector
                    def _(v):
                        for tt in range(NT):
                            v.wait_ge(sem["pe"], b_pe + 2 * NFH + tt + 1)
                            v.scalar_tensor_tensor(out=RB[:, tt, :], in0=psY[tt % 2][:],
                                                   scalar=gt[:, tt, e:e + 1],
                                                   in1=RB[:, tt, :],
                                                   op0=ALU.mult, op1=ALU.add
                                                   ).then_inc(sem["dve"], 1)
                        cnt["dve"] += NT

        gTt = carve(DA, 0, 4096, BF16, p1=8)
        b2b = carve(DA, 4096, 2048, BF16, p1=8)
        b_pe, b_gd, b_dve = cnt["pe"], cnt["gd"], cnt["dve"]
        with (PS("psG", [8, 2048], BF16) as psG,
              PS("psBi0", [128, 1024], F32) as psBi0,
              PS("psBi1", [128, 1024], F32) as psBi1,
              BLK("b2bias") as blk):
            psBi = [psBi0, psBi1]

            @blk.gpsimd
            def _(g):
                inc(g.dma_start(out=b2b, in_=b2_h), "gd", 16)

            @blk.tensor
            def _(t):
                for tt in range(NT):
                    mm = t.transpose(psG[:, tt * 128:(tt + 1) * 128], gbf[:, tt, :],
                                     idb[:])
                mm.then_inc(sem["pe"], 1)
                cnt["pe"] += 1
                t.wait_ge(sem["dve"], b_dve + 1)
                t.wait_ge(sem["gd"], b_gd + 16)
                for tt in range(NT):
                    if tt >= 2:
                        t.wait_ge(sem["dve"], b_dve + tt)
                    pb = psBi[tt % 2]
                    t.matmul(pb[:, 0:512], gTt[:, tt * 128:(tt + 1) * 128],
                             b2b[:, 0:512], start=True, stop=True)
                    mm = t.matmul(pb[:, 512:1024], gTt[:, tt * 128:(tt + 1) * 128],
                                  b2b[:, 512:1024], start=True, stop=True)
                    mm.then_inc(sem["pe"], 1)
                cnt["pe"] += NT

            @blk.vector
            def _(v):
                v.wait_ge(sem["pe"], b_pe + 1)
                v.tensor_copy(out=gTt[:, :], in_=psG[:, :]).then_inc(sem["dve"], 1)
                cnt["dve"] += 1
                for tt in range(NT):
                    v.wait_ge(sem["pe"], b_pe + tt + 2)
                    v.tensor_tensor(out=RB[:, tt, :], in0=RB[:, tt, :],
                                    in1=psBi[tt % 2][:], op=ALU.add
                                    ).then_inc(sem["dve"], 1)
                cnt["dve"] += NT

    def emit_res0():
        xst = carve(DA, 0, 8192, F32, D)
        b_sd, b_sd2, b_dve = cnt["sd"], cnt["sd2"], cnt["dve"]
        with BLK("res0") as blk:
            @blk.sync
            def _(s):
                for tt in range(NT):
                    if tt >= 2:
                        s.wait_ge(sem["dve"], b_dve + tt - 1)
                    sp = "sd" if tt % 2 == 0 else "sd2"
                    inc(s.dma_start(out=xst[:, tt % 2, :], in_=x_v[:, tt, :]), sp, 16)
                s.wait_ge(sem["sd"], cnt["sd"])
                s.wait_ge(sem["sd2"], cnt["sd2"])

            @blk.vector
            def _(v):
                for tt in range(NT):
                    sp, bb = ("sd", b_sd) if tt % 2 == 0 else ("sd2", b_sd2)
                    v.wait_ge(sem[sp], bb + (tt // 2 + 1) * 16)
                    v.tensor_tensor(out=XB[:, tt, :], in0=xst[:, tt % 2, :],
                                    in1=RB[:, tt, :], op=ALU.add).then_inc(sem["dve"], 1)
            cnt["dve"] += NT

    def emit_attn1():
        qg = carve(DA, 0, 4096, BF16, 128)
        kg = carve(DA, 4096, 4096, BF16, 128)
        vg = carve(DA, 8192, 4096, BF16, 128)
        qfm = carve(DA, 12288, 4096, BF16)
        kfm = carve(DA, 16384, 4096, BF16)
        Afm = carve(DA, 20480, 8192, BF16, p1=64)
        P_sb = carve(DA, 0, 4096, BF16)
        PT_sb = carve(DA, 4096, 4096, BF16, 128)
        t1 = carve(DA, 20480, 2048, BF16)
        t2 = carve(DA, 22528, 2048, BF16)
        t3 = carve(DA, 24576, 2048, BF16)
        borow1 = carve(DA, 28672, 2048, BF16, p1=1)
        wqkv_s = carve(WS, 0, 6144, BF16, 384)
        wo1_s = carve(DA, 4096, 4096, BF16, 1024, p1=64)

        with BLK("a1pre") as blk:
            @blk.gpsimd
            def _(g):
                inc(g.dma_start(out=wqkv_s, in_=wqkv_h[0]), "gd", 16)
                inc(g.dma_start(out=brow[0:1, 0:384], in_=bqkv_h[0:1, :]), "gd", 16)
                g.wait_ge(sem["gd"], cnt["gd"])

        for p in range(NPAIR):
            pk = (p % 2) * 384
            b_pe, b_act, b_gd = cnt["pe"], cnt["act"], cnt["gd"]
            with (PS("psQ0", [128, 512], F32) as psQ0,
                  PS("psQ1", [128, 512], F32) as psQ1,
                  BLK("a1a") as blk):
                psQ = [psQ0, psQ1]

                @blk.gpsimd
                def _(g):
                    if p + 1 < NPAIR:
                        g.wait_ge(sem["pe"], b_pe + NT)
                        inc(g.dma_start(out=wqkv_s, in_=wqkv_h[p + 1]), "gd", 16)
                        inc(g.dma_start(out=brow[0:1, 384 - pk:768 - pk],
                                        in_=bqkv_h[p + 1:p + 2, :]), "gd", 16)
                    g.wait_ge(sem["gd"], cnt["gd"])

                @blk.tensor
                def _(t):
                    for tt in range(NT):
                        if tt >= 2:
                            t.wait_ge(sem["act"], b_act + tt - 1)
                        pq = psQ[tt % 2][:, 0:384]
                        for dc in range(ND):
                            t.matmul(pq, XN[:, dc, tt * 128:(tt + 1) * 128],
                                     wqkv_s[:, dc, :], start=(dc == 0), stop=False)
                        mm = t.matmul(pq, ones_b[:], brow[0:1, pk:pk + 384],
                                      start=False, stop=True)
                        mm.then_inc(sem["pe"], 1)
                    cnt["pe"] += NT

                @blk.scalar
                def _(sc):
                    for tt in range(NT):
                        sc.wait_ge(sem["pe"], b_pe + tt + 1)
                        pq = psQ[tt % 2][:, 0:384]
                        sc.activation(out=qg[:, tt, :], in_=pq[:, 0:128], func=AF.Copy)
                        sc.activation(out=kg[:, tt, :], in_=pq[:, 128:256], func=AF.Copy)
                        sc.activation(out=vg[:, tt, :], in_=pq[:, 256:384], func=AF.Copy
                                      ).then_inc(sem["act"], 1)
                    cnt["act"] += NT

            with BLK("a1b") as blk:
                @blk.vector
                def _(v):
                    for buf in (qg, kg):
                        xe = buf[:, :, 0:128:2]
                        xo = buf[:, :, 1:128:2]
                        t1v = t1[:, :].rearrange("p (a b) -> p a b", b=64)
                        t2v = t2[:, :].rearrange("p (a b) -> p a b", b=64)
                        t3v = t3[:, :].rearrange("p (a b) -> p a b", b=64)
                        v.tensor_tensor(out=t1v, in0=xe, in1=cosb[:], op=ALU.mult)
                        v.tensor_tensor(out=t2v, in0=xo, in1=sinb[:], op=ALU.mult)
                        mm = v.tensor_tensor(out=t3v, in0=xo, in1=cosb[:],
                                             op=ALU.mult)
                        CH(v, mm)
                        mm = v.tensor_tensor(out=xo, in0=xe, in1=sinb[:], op=ALU.mult)
                        CH(v, mm)
                        mm = v.tensor_tensor(out=xo, in0=xo, in1=t3v, op=ALU.add)
                        CH(v, mm)
                        v.tensor_tensor(out=xe, in0=t1v, in1=t2v, op=ALU.subtract)

            b_pe, b_dve = cnt["pe"], cnt["dve"]
            with (PS("psT0", [128, 1024], BF16) as psT0,
                  PS("psT1", [128, 1024], BF16) as psT1,
                  BLK("a1c") as blk):
                psTq = [psT0, psT1]

                @blk.tensor
                def _(t):
                    for b in range(4):
                        if b >= 2:
                            t.wait_ge(sem["dve"], b_dve + (b - 1) * 2)
                        pt = psTq[b % 2]
                        for j in range(4):
                            tt = b * 4 + j
                            t.transpose(pt[:, j * 256:j * 256 + 128], qg[:, tt, :],
                                        idb[:])
                            mm = t.transpose(pt[:, j * 256 + 128:j * 256 + 256],
                                             kg[:, tt, :], idb[:])
                        mm.then_inc(sem["pe"], 1)
                    cnt["pe"] += 4

                @blk.vector
                def _(v):
                    for b in range(4):
                        v.wait_ge(sem["pe"], b_pe + b + 1)
                        pt = psTq[b % 2].rearrange("p (a two t) -> p a two t", two=2,
                                                   t=128)
                        v.tensor_copy(out=qfm[:, b * 512:(b + 1) * 512].rearrange(
                            "p (a t) -> p a t", t=128), in_=pt[:, :, 0, :])
                        v.tensor_copy(out=kfm[:, b * 512:(b + 1) * 512].rearrange(
                            "p (a t) -> p a t", t=128), in_=pt[:, :, 1, :]
                        ).then_inc(sem["dve"], 2)
                    cnt["dve"] += 8

            b_pe, b_act, b_dve, b_dve2 = cnt["pe"], cnt["act"], cnt["dve"], cnt["dve2"]
            with (PS("psS", [128, 2048], F32) as psS,
                  PS("psPT0", [128, 1024], BF16) as psPT0,
                  PS("psPT1", [128, 1024], BF16) as psPT1,
                  PS("psPV", [64, 512], F32) as psPV,
                  BLK("a1d") as blk):
                psPT = [psPT0, psPT1]
                b_dve3 = cnt["dve3"]
                seq = [(hp, qt) for hp in range(2) for qt in range(NT)]
                n_it = len(seq)
                nbat = [(qt // 8) + 1 for hp, qt in seq]
                PE_sc, PE_pv, GB0 = [], [], []
                acc_pe, acc_gb = b_pe, 0
                for i in range(n_it):
                    PE_sc.append(acc_pe + 1)
                    PE_pv.append(acc_pe + 1 + nbat[i] + 1)
                    GB0.append(acc_gb)
                    acc_pe += 2 + nbat[i]
                    acc_gb += nbat[i]
                n_gb = acc_gb

                @blk.tensor
                def _(t):
                    for i, (hp, qt) in enumerate(seq):
                        h0 = 64 * hp
                        nk = (qt + 1) * 128
                        if i > 0:
                            t.wait_ge(sem["act"], b_act + i)
                        for c0 in range(0, nk, 512):
                            cw = min(512, nk - c0)
                            mm = t.matmul(psS[:, c0:c0 + cw],
                                          qfm[h0:h0 + 64, qt * 128:(qt + 1) * 128],
                                          kfm[h0:h0 + 64, c0:c0 + cw],
                                          start=True, stop=True)
                        mm.then_inc(sem["pe"], 1)
                        t.wait_ge(sem["dve"], b_dve + 2 * i + 2)
                        for b in range(nbat[i]):
                            gb = GB0[i] + b
                            if gb >= 2:
                                t.wait_ge(sem["dve2"], b_dve2 + gb - 1)
                            k0, k1 = b * 8, min((b + 1) * 8, qt + 1)
                            for kt in range(k0, k1):
                                mm = t.transpose(
                                    psPT[gb % 2][:, (kt - k0) * 128:(kt - k0 + 1) * 128],
                                    P_sb[:, kt * 128:(kt + 1) * 128], idb[:])
                            mm.then_inc(sem["pe"], 1)
                        if i > 0:
                            t.wait_ge(sem["dve3"], b_dve3 + i)
                        for b in range(nbat[i]):
                            t.wait_ge(sem["dve2"], b_dve2 + GB0[i] + b + 1)
                            k0, k1 = b * 8, min((b + 1) * 8, qt + 1)
                            for kt in range(k0, k1):
                                mm = t.matmul(psPV[:, 0:128], vg[:, kt, h0:h0 + 64],
                                              PT_sb[:, kt, :], start=(kt == 0),
                                              stop=(kt == qt))
                        mm.then_inc(sem["pe"], 1)
                    cnt["pe"] += sum(2 + nbat[i] for i in range(n_it))

                @blk.vector
                def _(v):
                    for i, (hp, qt) in enumerate(seq):
                        nk = (qt + 1) * 128
                        v.wait_ge(sem["pe"], PE_sc[i])
                        mm = v.tensor_tensor(out=psS[:, qt * 128:(qt + 1) * 128],
                                             in0=psS[:, qt * 128:(qt + 1) * 128],
                                             in1=tri[:], op=ALU.add)
                        CH(v, mm)
                        mm = v.reduce_max(out=stats[:, 0:1], in_=psS[:, 0:nk],
                                          axis=AX.X, negate=True)
                        CH(v, mm)
                        mm = v.tensor_scalar(out=stats[:, 1:2], in0=stats[:, 0:1],
                                             scalar1=0.125, scalar2=None, op0=ALU.mult)
                        inc(mm, "dve")
                        v.wait_ge(sem["act"], b_act + i + 1)
                        mm = v.reciprocal(out=stats[:, 3:4], in_=stats[:, 2:3])
                        CH(v, mm)
                        mm = v.tensor_scalar(out=P_sb[:, 0:nk], in0=P_sb[:, 0:nk],
                                             scalar1=stats[:, 3:4], scalar2=None,
                                             op0=ALU.mult)
                        inc(mm, "dve")
                        for b in range(nbat[i]):
                            gb = GB0[i] + b
                            v.wait_ge(sem["pe"], PE_sc[i] + b + 1)
                            k0, k1 = b * 8, min((b + 1) * 8, qt + 1)
                            v.tensor_copy(
                                out=PT_sb[:, k0:k1, :],
                                in_=psPT[gb % 2][:, 0:(k1 - k0) * 128].rearrange(
                                    "p (a t) -> p a t", t=128)
                            ).then_inc(sem["dve2"], 1)
                        v.wait_ge(sem["pe"], PE_pv[i])
                        v.tensor_copy(out=Afm[:, hp * 2048 + qt * 128:
                                              hp * 2048 + (qt + 1) * 128],
                                      in_=psPV[:, 0:128]).then_inc(sem["dve3"], 1)
                    cnt["dve2"] += n_gb
                    cnt["dve3"] += n_it

                @blk.scalar
                def _(sc):
                    for i, (hp, qt) in enumerate(seq):
                        nk = (qt + 1) * 128
                        sc.wait_ge(sem["dve"], b_dve + 2 * i + 1)
                        if i > 0:
                            sc.wait_ge(sem["pe"], PE_pv[i - 1] - 1)
                        sc.activation(out=P_sb[:, 0:nk], in_=psS[:, 0:nk], func=AF.Exp,
                                      scale=0.125, bias=stats[:, 1:2],
                                      accum_out=stats[:, 2:3]).then_inc(sem["act"], 1)
                    cnt["act"] += n_it

            b_pe2, b_dve3, b_gd = cnt["pe2"], cnt["dve3"], cnt["gd"]
            with (PS("psF0", [128, 1024], F32) as psF0,
                  PS("psF1", [128, 1024], F32) as psF1,
                  BLK("a1e") as blk):
                psF = [psF0, psF1]

                @blk.gpsimd
                def _(g):
                    inc(g.dma_start(
                        out=wo1_s,
                        in_=wout_h[p * 128:(p + 1) * 128, :].rearrange(
                            "(h a) f -> a h f", h=2)), "gd", 16)
                    if p == 0:
                        inc(g.dma_start(out=borow1, in_=bout_h), "gd", 16)
                    g.wait_ge(sem["gd"], cnt["gd"])

                @blk.tensor
                def _(t):
                    t.wait_ge(sem["gd"], cnt["gd"])
                    for tt in range(NT):
                        if tt >= 2:
                            t.wait_ge(sem["dve3"], b_dve3 + tt - 1)
                        pf = psF[tt % 2]
                        for nb in range(2):
                            for hp in range(2):
                                mm = t.matmul(pf[:, nb * 512:(nb + 1) * 512],
                                              Afm[:, hp * 2048 + tt * 128:
                                                  hp * 2048 + (tt + 1) * 128],
                                              wo1_s[:, hp, nb * 512:(nb + 1) * 512],
                                              start=(hp == 0),
                                              stop=(hp == 1 and p != 0))
                            if p == 0:
                                mm = t.matmul(pf[:, nb * 512:(nb + 1) * 512],
                                              ones_b[:],
                                              borow1[0:1, nb * 512:(nb + 1) * 512],
                                              start=False, stop=True)
                        mm.then_inc(sem["pe2"], 1)
                    cnt["pe2"] += NT

                @blk.vector
                def _(v):
                    for tt in range(NT):
                        v.wait_ge(sem["pe2"], b_pe2 + tt + 1)
                        pf = psF[tt % 2]
                        if p == 0:
                            v.tensor_copy(out=RB[:, tt, :], in_=pf[:]
                                          ).then_inc(sem["dve3"], 1)
                        else:
                            v.tensor_tensor(out=RB[:, tt, :], in0=RB[:, tt, :],
                                            in1=pf[:], op=ALU.add
                                            ).then_inc(sem["dve3"], 1)
                    cnt["dve3"] += NT

    _ffn1_state = {"ewready": {}}

    def emit_route1():
        gws = carve(WS, 4096, 128, BF16, 8)
        gwsf = carve(WS, 0, 256, F32, 8)
        gws_lo = carve(WS, 4224, 128, BF16, 8)

        b_pe, b_gd, b_dve0 = cnt["pe"], cnt["gd"], cnt["dve"]
        b_aux = cnt["aux"]
        with (PS("psR0", [128, 512], F32) as psR0,
              PS("psR1", [128, 512], F32) as psR1,
              BLK("route1") as blk):
            psR = [psR0, psR1]

            @blk.gpsimd
            def _(g):
                inc(g.dma_start(out=gwsf,
                                in_=gw_h.rearrange("(c p) e -> p c e", p=128)),
                    "gd", 16)
                inc(g.dma_start(out=brow[0:1, 0:8], in_=gb_h), "gd", 16)
                g.wait_ge(sem["gd"], cnt["gd"])

            @blk.tensor
            def _(t):
                t.wait_ge(sem["gd"], b_gd + 32)
                t.wait_ge(sem["aux"], b_aux + 1)
                for tt in range(NT):
                    if tt >= 2:
                        t.wait_ge(sem["dve"], b_dve0 + tt - 1)
                    pr = psR[tt % 2][:, 0:8]
                    for dc in range(ND):
                        t.matmul(pr, XN[:, dc, tt * 128:(tt + 1) * 128], gws[:, dc, :],
                                 start=(dc == 0), stop=False)
                    for dc in range(ND):
                        t.matmul(pr, PARK1[:, dc, tt * 128:(tt + 1) * 128],
                                 gws[:, dc, :], start=False, stop=False)
                    for dc in range(ND):
                        t.matmul(pr, XN[:, dc, tt * 128:(tt + 1) * 128],
                                 gws_lo[:, dc, :], start=False, stop=False)
                    mm = t.matmul(pr, ones_b[:], brow[0:1, 0:8], start=False, stop=True)
                    mm.then_inc(sem["pe"], 1)
                cnt["pe"] += NT

            b_act_r = cnt["act"]

            @blk.vector
            def _(v):
                v.wait_ge(sem["gd"], b_gd + 16)
                mm = v.tensor_copy(out=gws[:, :, :], in_=gwsf[:, :, :])
                CH(v, mm)
                mm = v.tensor_tensor(out=gws_lo[:, :, :], in0=gwsf[:, :, :],
                                     in1=gws[:, :, :], op=ALU.subtract)
                inc(mm, "aux")
                for tt in range(NT):
                    v.wait_ge(sem["pe"], b_pe + tt + 1)
                    pr = psR[tt % 2][:, 0:8]
                    mm = v.tensor_scalar(out=r8a[:], in0=pr, scalar1=L1FFN_EPS,
                                         scalar2=None, op0=ALU.max)
                    CH(v, mm)
                    mm = v.reduce_max(out=stats[:, 0:1], in_=r8a[:], axis=AX.X,
                                      negate=True)
                    inc(mm, "dve")
                    v.wait_ge(sem["act"], b_act_r + tt + 1)
                    mm = v.reciprocal(out=stats[:, 3:4], in_=stats[:, 2:3])
                    CH(v, mm)
                    mm = v.tensor_scalar(out=r8a[:], in0=r8b[:], scalar1=stats[:, 3:4],
                                         scalar2=None, op0=ALU.mult)
                    CH(v, mm)
                    mm = v.tensor_tensor(out=r8b[:], in0=r8a[:], in1=idx8[:],
                                         op=ALU.subtract)
                    CH(v, mm)
                    mm = v.reduce_max(out=stats[:, 4:5], in_=r8b[:], axis=AX.X,
                                      negate=True)
                    CH(v, mm)
                    mm = v.tensor_scalar(out=stats[:, 5:6], in0=stats[:, 4:5],
                                         scalar1=-1.0, scalar2=None, op0=ALU.mult)
                    CH(v, mm)
                    mm = v.tensor_scalar(out=r8c[:], in0=r8b[:], scalar1=stats[:, 5:6],
                                         scalar2=None, op0=ALU.is_ge)
                    CH(v, mm)
                    mm = v.scalar_tensor_tensor(out=r8c[:], in0=r8c[:], scalar=-2.0,
                                                in1=r8b[:], op0=ALU.mult, op1=ALU.add)
                    CH(v, mm)
                    mm = v.reduce_max(out=stats[:, 6:7], in_=r8c[:], axis=AX.X,
                                      negate=True)
                    CH(v, mm)
                    mm = v.tensor_scalar(out=stats[:, 7:8], in0=stats[:, 6:7],
                                         scalar1=-1.0, scalar2=None, op0=ALU.mult)
                    CH(v, mm)
                    mm = v.tensor_scalar(out=r8c[:], in0=r8b[:], scalar1=stats[:, 7:8],
                                         scalar2=None, op0=ALU.is_ge)
                    CH(v, mm)
                    mm = v.tensor_scalar(out=r8b[:], in0=r8a[:], scalar1=W_THRES,
                                         scalar2=None, op0=ALU.max)
                    CH(v, mm)
                    mm = v.tensor_tensor(out=r8b[:], in0=r8b[:], in1=r8c[:],
                                         op=ALU.mult)
                    CH(v, mm)
                    mm = v.reduce_sum(out=stats[:, 8:9], in_=r8b[:], axis=AX.X)
                    CH(v, mm)
                    mm = v.reciprocal(out=stats[:, 9:10], in_=stats[:, 8:9])
                    CH(v, mm)
                    mm = v.tensor_scalar(out=gt[:, tt, :], in0=r8b[:],
                                         scalar1=stats[:, 9:10], scalar2=0.5,
                                         op0=ALU.mult, op1=ALU.mult)
                    CH(v, mm)
                    mm = v.tensor_copy(out=gbf[:, tt, :], in_=gt[:, tt, :])
                    inc(mm, "dve2")

            b_dve_r = cnt["dve"] - NT

            @blk.scalar
            def _(sc):
                for tt in range(NT):
                    sc.wait_ge(sem["dve"], b_dve_r + tt + 1)
                    sc.activation(out=r8b[:], in_=r8a[:], func=AF.Exp,
                                  bias=stats[:, 0:1], accum_out=stats[:, 2:3]
                                  ).then_inc(sem["act"], 1)
            cnt["act"] += NT

    def emit_ffn1():
        ewready = _ffn1_state["ewready"]
        ewb = [carve(DA, 0, 16384, BF16, D), carve(DA, 16384, 16384, BF16, D)]
        for e in range(E):
            b_pe, b_dve, b_gd = cnt["pe"], cnt["dve"], cnt["gd"]
            with (PS("psY0", [128, 1024], F32) as psY0,
                  PS("psY1", [128, 1024], F32) as psY1,
                  BLK("ffn1") as blk):
                psY = [psY0, psY1]

                @blk.gpsimd
                def _(g):
                    if e == 0:
                        inc(g.dma_start(out=ewb[0], in_=ew_h[0]), "gd", 16)
                        ewready[0] = cnt["gd"]
                    if e + 1 < E:
                        inc(g.dma_start(out=ewb[(e + 1) % 2], in_=ew_h[e + 1]), "gd", 16)
                        ewready[e + 1] = cnt["gd"]

                @blk.tensor
                def _(t):
                    t.wait_ge(sem["gd"], ewready[e])
                    for tt in range(NT):
                        if tt >= 2:
                            t.wait_ge(sem["dve"], b_dve + tt - 1)
                        py = psY[tt % 2]
                        for nb in range(2):
                            for dc in range(ND):
                                mm = t.matmul(py[:, nb * 512:(nb + 1) * 512],
                                              XN[:, dc, tt * 128:(tt + 1) * 128],
                                              ewb[e % 2][:, dc, nb * 512:(nb + 1) * 512],
                                              start=(dc == 0), stop=(dc == ND - 1))
                        mm.then_inc(sem["pe"], 1)
                    cnt["pe"] += NT

                @blk.vector
                def _(v):
                    for tt in range(NT):
                        v.wait_ge(sem["pe"], b_pe + tt + 1)
                        v.scalar_tensor_tensor(out=RB[:, tt, :], in0=psY[tt % 2][:],
                                               scalar=gt[:, tt, e:e + 1],
                                               in1=RB[:, tt, :], op0=ALU.mult,
                                               op1=ALU.add).then_inc(sem["dve"], 1)
                    cnt["dve"] += NT

        cTt = carve(DA, 0, 4096, BF16, p1=8)
        ebb = carve(DA, 4096, 2048, BF16, p1=8)
        b_pe, b_gd, b_dve = cnt["pe"], cnt["gd"], cnt["dve"]
        with (PS("psG", [8, 2048], BF16) as psG,
              PS("psBi0", [128, 1024], F32) as psBi0,
              PS("psBi1", [128, 1024], F32) as psBi1,
              BLK("ebbias") as blk):
            psBi = [psBi0, psBi1]

            @blk.gpsimd
            def _(g):
                inc(g.dma_start(out=ebb, in_=eb_h), "gd", 16)

            @blk.tensor
            def _(t):
                for tt in range(NT):
                    mm = t.transpose(psG[:, tt * 128:(tt + 1) * 128], gbf[:, tt, :],
                                     idb[:])
                mm.then_inc(sem["pe"], 1)
                cnt["pe"] += 1
                t.wait_ge(sem["dve"], b_dve + 1)
                t.wait_ge(sem["gd"], b_gd + 16)
                for tt in range(NT):
                    if tt >= 2:
                        t.wait_ge(sem["dve"], b_dve + tt)
                    pb = psBi[tt % 2]
                    t.matmul(pb[:, 0:512], cTt[:, tt * 128:(tt + 1) * 128],
                             ebb[:, 0:512], start=True, stop=True)
                    mm = t.matmul(pb[:, 512:1024], cTt[:, tt * 128:(tt + 1) * 128],
                                  ebb[:, 512:1024], start=True, stop=True)
                    mm.then_inc(sem["pe"], 1)
                cnt["pe"] += NT

            @blk.vector
            def _(v):
                v.wait_ge(sem["pe"], b_pe + 1)
                v.tensor_copy(out=cTt[:, :], in_=psG[:, :]).then_inc(sem["dve"], 1)
                cnt["dve"] += 1
                for tt in range(NT):
                    v.wait_ge(sem["pe"], b_pe + tt + 2)
                    v.tensor_tensor(out=RB[:, tt, :], in0=RB[:, tt, :],
                                    in1=psBi[tt % 2][:], op=ALU.add
                                    ).then_inc(sem["dve"], 1)
                cnt["dve"] += NT

    def emit_final():
        b_dve = cnt["dve"]
        with BLK("final") as blk:
            @blk.vector
            def _(v):
                for tt in range(NT):
                    v.tensor_tensor(out=RB[:, tt, :], in0=XB[:, tt, :],
                                    in1=RB[:, tt, :], op=ALU.add).then_inc(sem["dve"], 1)
            cnt["dve"] += NT

            @blk.sync
            def _(s):
                s.wait_ge(sem["dve"], b_dve + NT)
                s.dma_start(out=y_v, in_=RB[:]).then_inc(sem["sd"], 16)
                cnt["sd"] += 16
                s.wait_ge(sem["sd"], cnt["sd"])

    def emit_debug():
        if not debug:
            return
        with BLK("dbg") as blk:
            @blk.gpsimd
            def _(g):
                inc(g.dma_start(out=dbg["xn"], in_=XN[:]), "gd", 16)
                g.wait_ge(sem["gd"], cnt["gd"])

            @blk.sync
            def _(s):
                inc(s.dma_start(out=dbg["rb"], in_=RB[:]), "sd", 16)
                inc(s.dma_start(out=dbg["xb"], in_=XB[:]), "sd", 16)
                s.wait_ge(sem["sd"], cnt["sd"])

    order = PHASES[:PHASES.index(upto) + 1]
    for ph in order:
        if ph == "ln0":
            emit_ln("hbm", PARK0)
        elif ph == "attn0":
            emit_attn0()
        elif ph == "ffn0":
            emit_ffn0()
        elif ph == "res0":
            emit_res0()
        elif ph == "ln1":
            emit_ln("sbuf", PARK1)
        elif ph == "attn1":
            emit_route1()
            emit_attn1()
        elif ph == "ffn1":
            emit_ffn1()
        elif ph == "final":
            emit_final()
    emit_debug()
    if upto != "final":
        with BLK("partial_out") as blk:
            @blk.sync
            def _(s):
                s.dma_start(out=y_v, in_=RB[:]).then_inc(sem["sd"], 16)
                cnt["sd"] += 16
                s.wait_ge(sem["sd"], cnt["sd"])
    es.close()
    return nc


def prep_shared(i):
    f = np.ascontiguousarray
    out = {}
    WqT = i["L0_attn_Wq"].T
    WkT = i["L0_attn_Wk"].T
    WvT = i["L0_attn_Wv"].T
    out["wq_t"] = f(WqT.reshape(ND, 128, NPAIR, 128).transpose(2, 1, 0, 3))
    kp = WkT.reshape(ND, 128, NPAIR, 128).transpose(2, 1, 0, 3)
    vp = WvT.reshape(ND, 128, NPAIR, 128).transpose(2, 1, 0, 3)
    out["wkv_t"] = f(np.concatenate([kp, vp], axis=3))
    out["bq_t"] = f(i["L0_attn_bq"].reshape(16, 64).T)
    out["bkv"] = f(np.concatenate([i["L0_attn_bk"].reshape(NPAIR, 128),
                                   i["L0_attn_bv"].reshape(NPAIR, 128)], axis=1))
    out["wo_t"] = f(i["L0_attn_Wo"].T)
    out["bo"] = f(i["L0_attn_bo"].reshape(1, D))
    out["sw_w"] = f(i["L0_ffn_switch_w"])
    out["sw_b"] = f(i["L0_ffn_switch_b"].reshape(1, E))
    out["w1_t"] = f(i["L0_ffn_w1"].reshape(E, ND, 128, NFC, 128)
                    .transpose(0, 3, 2, 1, 4))
    out["b1_t"] = f(i["L0_ffn_b1"].reshape(E, 32, 128).transpose(0, 2, 1))
    out["w2_t"] = f(i["L0_ffn_w2"].reshape(E, 2, NFH, 128, D).transpose(0, 1, 3, 2, 4))
    out["b2"] = f(i["L0_ffn_b2"])
    WqkvT = i["L1_attn_w_qkv"].T
    parts = [WqkvT[:, j * D:(j + 1) * D].reshape(ND, 128, NPAIR, 128)
             .transpose(2, 1, 0, 3) for j in range(3)]
    out["wqkv_t"] = f(np.concatenate(parts, axis=3))
    bq = i["L1_attn_b_qkv"]
    out["bqkv"] = f(np.concatenate([bq[j * D:(j + 1) * D].reshape(NPAIR, 128)
                                    for j in range(3)], axis=1))
    out["wout_t"] = f(i["L1_attn_w_out"].T)
    out["bout"] = f(i["L1_attn_b_out"].reshape(1, D))
    out["gw"] = f(i["L1_ffn_gate_W"])
    out["gb"] = f(i["L1_ffn_gate_b"].reshape(1, E))
    out["ew_t"] = f(i["L1_ffn_expert_W"].reshape(E, ND, 128, D).transpose(0, 2, 1, 3))
    out["eb"] = f(i["L1_ffn_expert_b"])
    out["idb"] = np.eye(128, dtype=BF)
    tm = np.where(np.tril(np.ones((128, 128), dtype=bool)), 0.0, NEG).astype(BF)
    out["trimask"] = tm
    inv_freq = (1.0 / 10000.0 ** (np.arange(0, 64, 2, dtype=np.float64) / 64.0))
    pos = np.arange(T, dtype=np.float64)
    fr = pos[:, None] * inv_freq[None, :]
    cosr = np.tile(np.cos(fr), (1, 2)).astype(np.float32)
    sinr = np.tile(np.sin(fr), (1, 2)).astype(np.float32)
    out["cosb"] = np.ascontiguousarray(
        cosr.reshape(NT, 128, 64).transpose(1, 0, 2)).astype(BF)
    out["sinb"] = np.ascontiguousarray(
        sinr.reshape(NT, 128, 64).transpose(1, 0, 2)).astype(BF)
    out["idx8"] = np.tile((np.arange(E, dtype=np.float32) * 1e-7)[None, :], (128, 1))
    for k, v in out.items():
        if v.dtype == np.float64:
            out[k] = v.astype(np.float32)
    return out


def kernel(**inputs):
    shared = prep_shared(inputs)
    x = np.asarray(inputs["L0_attn_x"])
    nc = build("final", debug=False)
    in_maps = []
    for c in range(8):
        m = dict(shared)
        m["x"] = np.ascontiguousarray(x[c])
        in_maps.append(m)
    res = run_bass_kernel_spmd(nc, in_maps, core_ids=list(range(8)))
    outs = []
    for c in range(8):
        y = np.asarray(res.results[c]["y"])
        outs.append(y)
    return np.stack(outs).astype(np.float32)
